# version 01 (final)
# baseline (speedup 1.0000x reference)
import math
import numpy as np
import ml_dtypes

import concourse.bass as bass
import concourse.mybir as mybir
from concourse.bass_utils import run_bass_kernel_spmd

F32 = mybir.dt.float32
BF16 = mybir.dt.bfloat16
AF = mybir.ActivationFunctionType
ALU = mybir.AluOpType
AXX = mybir.AxisListType.X

D = 2048
T = 2048
DFF = 8192
EPS_LN = 1e-5
EPS_RMS = 1.1920929e-07
ISQ_D = 1.0 / math.sqrt(2048.0)
ISQ_F = 1.0 / math.sqrt(8192.0)
NCH = 16


def _hadamard(n):
    h = np.array([[1.0]], dtype=np.float32)
    while h.shape[0] < n:
        h = np.block([[h, h], [h, -h]])
    return h


def host_constants():
    bf = ml_dtypes.bfloat16
    return dict(
        cH=_hadamard(128).astype(bf),
        cI=np.eye(128, dtype=np.float32).astype(bf),
        cTRIU=np.triu(np.ones((128, 128), dtype=np.float32)).astype(bf),
        cSTL=np.tril(np.ones((128, 128), dtype=np.float32), -1).astype(bf),
        cFONES=np.ones((1, 128), dtype=np.float32).astype(bf),
        cBIAS=np.tile(np.array([[1e-5, 1.1920929e-07, 1e-6, 0.0]], dtype=np.float32), (128, 1)),
    )


class Sem:
    def __init__(self, handle, name):
        self.h = handle
        self.name = name
        self.count = 0


class Op:
    __slots__ = ("engine", "waits", "fn", "incs", "tag")
    def __init__(self, engine, waits, fn, incs, tag):
        self.engine, self.waits, self.fn, self.incs, self.tag = engine, waits, fn, incs, tag


ENGINES = ("sync", "vector", "scalar", "tensor", "gpsimd")


class Plan:
    def __init__(self):
        self.ops = {e: [] for e in ENGINES}
        self.order = []

    def emit(self, engine, fn, waits=(), inc=None, tag=""):
        incs = []
        if inc is not None:
            sem, n = inc
            sem.count += n
            incs.append((sem, n))
        wd = {}
        for sv in waits:
            s, v = sv
            if v > 0:
                if id(s) not in wd or wd[id(s)][1] < v:
                    wd[id(s)] = (s, v)
        op = Op(engine, list(wd.values()), fn, incs, tag)
        self.ops[engine].append(op)
        self.order.append(op)
        return op

    def check(self):
        cum = {}
        cum_after = {}
        prods = {}
        for op in self.order:
            for sem, n in op.incs:
                cum[id(sem)] = cum.get(id(sem), 0) + n
                cum_after[(id(sem), id(op))] = cum[id(sem)]
                prods.setdefault(id(sem), []).append(op)
        idx = {id(op): i for i, op in enumerate(self.order)}
        succ = [[] for _ in self.order]
        indeg = [0] * len(self.order)

        def edge(a, b):
            succ[idx[id(a)]].append(idx[id(b)])
            indeg[idx[id(b)]] += 1

        for e in ENGINES:
            for a, b in zip(self.ops[e], self.ops[e][1:]):
                edge(a, b)
        for op in self.order:
            for sem, v in op.waits:
                sat = None
                for p in prods.get(id(sem), []):
                    if cum_after[(id(sem), id(p))] >= v:
                        sat = p
                        break
                if sat is None:
                    raise RuntimeError(
                        f"wait {sem.name}>={v} unsatisfiable (max {cum.get(id(sem), 0)}) at '{op.tag}'")
                edge(sat, op)
        from collections import deque
        q = deque(i for i in range(len(self.order)) if indeg[i] == 0)
        seen = 0
        while q:
            i = q.popleft()
            seen += 1
            for j in succ[i]:
                indeg[j] -= 1
                if indeg[j] == 0:
                    q.append(j)
        if seen != len(self.order):
            bad = [self.order[i].tag for i in range(len(self.order)) if indeg[i] > 0][:12]
            raise RuntimeError(f"plan deadlock; stuck: {bad}")


class Region:
    def __init__(self, base, cap, name):
        self.base, self.cap, self.name = base, cap, name
        self.top = 0
        self.marks = []

    def alloc(self, elems, align=64):
        self.top = (self.top + align - 1) // align * align
        off = self.base + self.top
        self.top += elems
        if self.top > self.cap:
            raise RuntimeError(f"region {self.name} overflow {self.top} > {self.cap}")
        return off

    def mark(self):
        self.marks.append(self.top)

    def release(self):
        self.top = self.marks.pop()


def build_nc(taps=(), stop_after=None):
    nc = bass.Bass("TRN2", target_bir_lowering=False, debug=False)
    P = Plan()

    x_in = nc.dram_tensor("x", [T, D], F32, kind="ExternalInput").ap()
    w_qkv = nc.dram_tensor("w_qkv", [3 * D, D], F32, kind="ExternalInput").ap()
    w_sw = nc.dram_tensor("w_sw", [2 * DFF, D], F32, kind="ExternalInput").ap()
    w_out = nc.dram_tensor("w_out", [D, DFF], F32, kind="ExternalInput").ap()
    w_up = nc.dram_tensor("w_up", [DFF, D], F32, kind="ExternalInput").ap()
    w_gate = nc.dram_tensor("w_gate", [DFF, D], F32, kind="ExternalInput").ap()
    w_down = nc.dram_tensor("w_down", [D, DFF], F32, kind="ExternalInput").ap()
    su_ug = nc.dram_tensor("su_ug", [128, 16], F32, kind="ExternalInput").ap()
    su_dp = nc.dram_tensor("su_dp", [128, 64], F32, kind="ExternalInput").ap()
    wsu_in = nc.dram_tensor("wsu", [128, 64], F32, kind="ExternalInput").ap()
    wsg_in = nc.dram_tensor("wsg", [128, 64], F32, kind="ExternalInput").ap()
    wsd_in = nc.dram_tensor("wsd", [128, 16], F32, kind="ExternalInput").ap()
    cH_d = nc.dram_tensor("cH", [128, 128], BF16, kind="ExternalInput").ap()
    cI_d = nc.dram_tensor("cI", [128, 128], BF16, kind="ExternalInput").ap()
    cTRIU_d = nc.dram_tensor("cTRIU", [128, 128], BF16, kind="ExternalInput").ap()
    cSTL_d = nc.dram_tensor("cSTL", [128, 128], BF16, kind="ExternalInput").ap()
    cFON_d = nc.dram_tensor("cFONES", [1, 128], BF16, kind="ExternalInput").ap()
    cB_d = nc.dram_tensor("cBIAS", [128, 4], F32, kind="ExternalInput").ap()

    out = nc.dram_tensor("out", [T, D], F32, kind="ExternalOutput").ap()

    Tqkv = nc.dram_tensor("Tqkv", [D, 3 * D], BF16, kind="Internal").ap()
    Tsw = nc.dram_tensor("Tsw", [D, 2 * DFF], BF16, kind="Internal").ap()
    Twout = nc.dram_tensor("Twout", [DFF, D], BF16, kind="Internal").ap()
    Tup = nc.dram_tensor("Tup", [D, DFF], BF16, kind="Internal").ap()
    Tgate = nc.dram_tensor("Tgate", [D, DFF], BF16, kind="Internal").ap()
    Tdown = nc.dram_tensor("Tdown", [DFF, D], BF16, kind="Internal").ap()
    X2S = nc.dram_tensor("X2S", [DFF, T], BF16, kind="Internal").ap()
    YFS = nc.dram_tensor("YFS", [D, T], BF16, kind="Internal").ap()
    ASW = nc.dram_tensor("ASW", [T, D], F32, kind="Internal").ap()

    tapd = {}
    for tname, tshape, tdt in taps:
        tapd[tname] = nc.dram_tensor("tap_" + tname, tshape, tdt, kind="ExternalOutput").ap()

    import contextlib
    es = contextlib.ExitStack()
    with es:
        mk = es.enter_context

        CAP = 99776
        AR = mk(nc.sbuf_tensor("AR", [128, CAP], BF16))
        RA = Region(0, 40960, "A")
        RB = Region(40960, 32768, "B")
        RC = Region(73728, CAP - 73728, "C")
        def raw_bf(off, elems):
            return AR[:, off:off + elems]

        def bfb(region, elems):
            off = region.alloc(elems)
            return AR[:, off:off + elems]

        def f32b(region, elems):
            off = region.alloc(2 * elems)
            return AR[:, off:off + 2 * elems].bitcast(F32)

        sH = mk(nc.sbuf_tensor("sH", [128, 128], BF16))
        sI = mk(nc.sbuf_tensor("sI", [128, 128], BF16))
        sTRIU = mk(nc.sbuf_tensor("sTRIU", [128, 128], BF16))
        sSTL = mk(nc.sbuf_tensor("sSTL", [128, 128], BF16))
        sFON = mk(nc.sbuf_tensor("sFON", [1, 128], BF16))
        sSUu = mk(nc.sbuf_tensor("sSUu", [128, 16], F32))
        sSUd = mk(nc.sbuf_tensor("sSUd", [128, 64], F32))
        sWSU = mk(nc.sbuf_tensor("sWSU", [128, 64], F32))
        sWSG = mk(nc.sbuf_tensor("sWSG", [128, 64], F32))
        sWSD = mk(nc.sbuf_tensor("sWSD", [128, 16], F32))
        STATS = mk(nc.sbuf_tensor("STATS", [128, NCH * 8], F32))
        PSB = mk(nc.sbuf_tensor("PSB", [1, 2 * D], BF16))
        DUMP = mk(nc.sbuf_tensor("DUMP", [128, 1024], BF16))
        sCB = mk(nc.sbuf_tensor("sCB", [128, 4], F32))
        SS = mk(nc.sbuf_tensor("SS", [128, 64], F32))

        pb = [mk(nc.psum_tensor(f"pb{i}", [128, 512], F32)) for i in range(8)]
        def pba(i):
            return pb[i].ap()
        def pbf(i):
            return pb[i].ap().bitcast(BF16)

        s_ld = Sem(mk(nc.semaphore("s_ld")), "s_ld")
        s_wr = Sem(mk(nc.semaphore("s_wr")), "s_wr")
        s_pe = Sem(mk(nc.semaphore("s_pe")), "s_pe")
        s_act = Sem(mk(nc.semaphore("s_act")), "s_act")
        s_dve = Sem(mk(nc.semaphore("s_dve")), "s_dve")
        s_pool = Sem(mk(nc.semaphore("s_pool")), "s_pool")
        def _sem(name):
            return Sem(mk(nc.semaphore(name)), name)
        c_const = _sem("c_const")
        c_pin = [_sem("c_pin0"), _sem("c_pin1")]
        c_x = [_sem("c_x0"), _sem("c_x1")]
        c_wq = [_sem("c_wq0"), _sem("c_wq1")]
        c_up = [_sem("c_up0"), _sem("c_up1")]
        c_gp = [_sem("c_gp0"), _sem("c_gp1")]
        c_yf = _sem("c_yf")
        c_us = [_sem("c_us0"), _sem("c_us1")]
        c_gs = [_sem("c_gs0"), _sem("c_gs1")]
        c_sg4 = [_sem("c_sg40"), _sem("c_sg41")]
        c_x2 = _sem("c_x2")
        c_sg5 = [_sem("c_sg50"), _sem("c_sg51")]
        c_xs = [_sem("c_xs0"), _sem("c_xs1")]
        c_as = [_sem("c_as0"), _sem("c_as1")]
        w_T = _sem("w_T")
        w_yfs = _sem("w_yfs")
        w_x2s = _sem("w_x2s")
        w_asw = _sem("w_asw")
        wr_out = _sem("wr_out")

        def LD(fn, waits=(), tag="", sem=None):
            return P.emit("sync", fn, waits, (sem if sem is not None else s_ld, 16), tag)
        def WR(fn, waits=(), tag="", sem=None):
            return P.emit("gpsimd", fn, waits, (sem if sem is not None else s_wr, 16), tag)
        def PE(fn, waits=(), inc=True, tag=""):
            return P.emit("tensor", fn, waits, (s_pe, 1) if inc else None, tag)
        def ACT(fn, waits=(), inc=True, tag=""):
            return P.emit("scalar", fn, waits, (s_act, 1) if inc else None, tag)
        def DVE(fn, waits=(), inc=True, tag=""):
            return P.emit("vector", fn, waits, (s_dve, 1) if inc else None, tag)
        def POOL(fn, waits=(), inc=True, tag=""):
            return P.emit("gpsimd", fn, waits, (s_pool, 1) if inc else None, tag)
        def TAP(name, src, waits, tag=""):
            if name in tapd:
                WR(lambda e, s=src, d=tapd[name]: e.dma_start(out=d, in_=s), waits=waits, tag="tap." + tag)

        bank_act = [0] * 8
        bank_dve = [0] * 8

        for dst, src, tg in ((sH[:], cH_d, "cH"), (sI[:], cI_d, "cI"), (sTRIU[:], cTRIU_d, "cT"),
                             (sSTL[:], cSTL_d, "cS"), (sFON[:], cFON_d, "cO"), (sSUu[:], su_ug, "su"),
                             (sSUd[:], su_dp, "sd"), (sWSU[:], wsu_in, "wu"), (sWSG[:], wsg_in, "wg"),
                             (sWSD[:], wsd_in, "wd"), (sCB[:], cB_d, "cb")):
            LD(lambda e, d=dst, s=src: e.dma_start(out=d, in_=s), tag="c." + tg, sem=c_const)
        C_LD = c_const.count

        RA.mark()
        RB.mark()
        pin = [f32b(RA, 8192) for _ in range(2)]
        pcast = [bfb(RB, 8192) for _ in range(2)]
        pstg = bfb(RB, 16384)
        stp = dict(cast=[0, 0], tr=[0, 0], ev=[0, 0], stgwr=0)

        def prep_weight(W, O, C, Tdst, scale_col=None, scale_const=None, tag=""):
            G = 256 if C == 2048 else 128
            sgr = 2
            n_oc = C // 128
            per = G // 128
            for g in range(O // G):
                b = g % 2
                gh = g % sgr
                src = W[g * G:(g + 1) * G, :].rearrange("(n p) m -> p n m", p=128)
                dstf = pin[b][:, 0:per * C].rearrange("p (n m) -> p n m", n=per)
                LD(lambda e, s=src, d2=dstf: e.dma_start(out=d2, in_=s),
                   waits=[(s_dve, stp["cast"][b]), (c_const, C_LD)], tag=f"{tag}.ld{g}",
                   sem=c_pin[b])
                ld_at = c_pin[b].count
                for tix in range(per):
                    i0 = tix * C
                    fin = pin[b][:, i0:i0 + C]
                    fout = pcast[b][:, i0:i0 + C]
                    w2 = [(c_pin[b], ld_at), (s_pe, stp["tr"][b])]
                    if scale_col is not None:
                        col = scale_col[:, g * per + tix: g * per + tix + 1]
                        DVE(lambda e, a=fin, o2=fout, c=col: e.tensor_scalar(
                            out=o2, in0=a, scalar1=c, scalar2=scale_const, op0=ALU.mult, op1=ALU.mult),
                            waits=w2, tag=f"{tag}.cs{g}.{tix}")
                    else:
                        DVE(lambda e, a=fin, o2=fout: e.tensor_copy(out=o2, in_=a),
                            waits=w2, tag=f"{tag}.cs{g}.{tix}")
                stp["cast"][b] = s_dve.count
                cast_at = s_dve.count
                for ch in range(n_oc):
                    bank = ch % 2
                    w3 = [(s_dve, cast_at), (s_act, stp["ev"][bank])]
                    if ch == 0 and gh == 0:
                        w3.append((w_T, stp["stgwr"]))
                    for tix in range(per):
                        def tr(e, tix=tix, ch=ch, bank=bank, b=b):
                            return e.transpose(pbf(bank)[:, tix * 128:(tix + 1) * 128],
                                               pcast[b][:, tix * C + ch * 128: tix * C + ch * 128 + 128],
                                               sI[:])
                        PE(tr, waits=(w3 if tix == 0 else ()), inc=(tix == per - 1),
                           tag=f"{tag}.tr{g}.{ch}.{tix}")
                    tr_at = s_pe.count
                    ACT(lambda e, ch=ch, bank=bank, gh=gh: e.copy(
                            out=pstg[:, ch * sgr * G + gh * G: ch * sgr * G + (gh + 1) * G],
                            in_=pbf(bank)[:, 0:G]),
                        waits=[(s_pe, tr_at)], tag=f"{tag}.ev{g}.{ch}")
                    stp["ev"][bank] = s_act.count
                stp["tr"][b] = s_pe.count
                ev_at = s_act.count
                if gh == sgr - 1:
                    g0 = g - (sgr - 1)
                    dstw = Tdst[:, g0 * G:(g0 + sgr) * G].rearrange("(ch p) m -> p ch m", p=128)
                    srcw = pstg[:, 0:n_oc * sgr * G].rearrange("p (ch m) -> p ch m", ch=n_oc)
                    WR(lambda e, s2=srcw, d2=dstw: e.dma_start(out=d2, in_=s2),
                       waits=[(s_act, ev_at)], tag=f"{tag}.wr{g}", sem=w_T)
                    stp["stgwr"] = w_T.count

        prep_weight(w_qkv, 3 * D, D, Tqkv, tag="Pq")
        PREP_QKV_WR = w_T.count
        TAP("Tqkv", Tqkv[0:128, 0:6144], [(w_T, PREP_QKV_WR)], "Tqkv")
        prep_weight(w_up, DFF, D, Tup, sWSU, ISQ_D, tag="Pu")
        prep_weight(w_gate, DFF, D, Tgate, sWSG, ISQ_D, tag="Pg")
        PREP_UG_WR = w_T.count
        prep_weight(w_sw, 2 * DFF, D, Tsw, tag="Ps")
        prep_weight(w_out, D, DFF, Twout, tag="Po")
        PREP_SWO_WR = w_T.count
        TAP("Twout0", Twout[0:128, :], [(w_T, PREP_SWO_WR)], "Twout0")
        TAP("Twout33", Twout[4224:4352, :], [(w_T, PREP_SWO_WR)], "Twout33")
        prep_weight(w_down, D, DFF, Tdown, sWSD, ISQ_F, tag="Pd")
        PREP_ALL_WR = w_T.count
        RA.release()
        RB.release()
        bank_act[0] = stp["ev"][0]
        bank_act[1] = stp["ev"][1]

        RB.mark()
        XF_off = RB.alloc(16 * 2048)
        XF = AR[:, XF_off:XF_off + 16 * 2048]
        def XFt(ch):
            return XF[:, ch * 2048:(ch + 1) * 2048]

        RC.mark()
        xin = [f32b(RC, 2048) for _ in range(2)]
        xnb = [bfb(RC, 2048) for _ in range(2)]
        stx = dict(xin=[0, 0], xnb=[0, 0])

        for tb in range(NCH):
            b = tb % 2
            LD(lambda e, tb=tb, b=b: e.dma_start(out=xin[b][:], in_=x_in[tb * 128:(tb + 1) * 128, :]),
               waits=[(s_dve, stx["xin"][b])], tag=f"x.ld{tb}", sem=c_x[b])
            ld_at = c_x[b].count
            DVE(lambda e, b=b: e.reduce_sum(out=SS[:, 0:1], in_=xin[b][:], axis=AXX),
                waits=[(c_x[b], ld_at)], tag=f"x.sum{tb}")
            sum_at = s_dve.count
            ACT(lambda e, b=b: e.activation(out=DUMP[:], in_=xin[b][:, 0:1024], func=AF.Square,
                                            accum_out=SS[:, 1:2]),
                waits=[(c_x[b], ld_at), (s_dve, sum_at)], tag=f"x.sq{tb}a")
            ACT(lambda e, b=b: e.activation(out=DUMP[:], in_=xin[b][:, 1024:2048], func=AF.Square,
                                            accum_out=SS[:, 11:12]),
                waits=[(s_act, s_act.count)], tag=f"x.sq{tb}b")
            sq_at = s_act.count
            DVE(lambda e, tb=tb: e.tensor_scalar_mul(out=STATS[:, tb * 8:tb * 8 + 1], in0=SS[:, 0:1],
                                                     scalar1=1.0 / 2048.0),
                waits=[(s_act, sq_at), (s_dve, s_dve.count)], tag=f"x.m{tb}")
            DVE(lambda e, tb=tb: e.tensor_tensor(out=SS[:, 2:3], in0=STATS[:, tb * 8:tb * 8 + 1],
                                                 in1=STATS[:, tb * 8:tb * 8 + 1], op=ALU.mult),
                waits=[(s_dve, s_dve.count)], tag=f"x.m2{tb}")
            DVE(lambda e: e.tensor_tensor(out=SS[:, 12:13], in0=SS[:, 1:2], in1=SS[:, 11:12], op=ALU.add),
                waits=[(s_dve, s_dve.count)], tag=f"x.ssum{tb}")
            DVE(lambda e: e.tensor_scalar_mul(out=SS[:, 3:4], in0=SS[:, 12:13], scalar1=1.0 / 2048.0),
                waits=[(s_dve, s_dve.count)], tag=f"x.ms{tb}")
            DVE(lambda e: e.tensor_tensor(out=SS[:, 4:5], in0=SS[:, 3:4], in1=SS[:, 2:3], op=ALU.subtract),
                waits=[(s_dve, s_dve.count)], tag=f"x.v{tb}")
            var_at = s_dve.count
            ACT(lambda e: e.activation(out=SS[:, 5:6], in_=SS[:, 4:5], func=AF.Ln, bias=sCB[:, 0:1]),
                waits=[(s_dve, var_at)], tag=f"x.l1{tb}")
            l1 = s_act.count
            ACT(lambda e, tb=tb: e.activation(out=STATS[:, tb * 8 + 1:tb * 8 + 2], in_=SS[:, 5:6],
                                              func=AF.Exp, scale=-0.5),
                waits=[(s_act, l1)], tag=f"x.inv{tb}")
            inv_at = s_act.count
            DVE(lambda e, tb=tb: e.tensor_tensor(out=SS[:, 6:7], in0=SS[:, 4:5],
                                                 in1=STATS[:, tb * 8 + 1:tb * 8 + 2], op=ALU.mult),
                waits=[(s_act, inv_at), (s_dve, s_dve.count)], tag=f"x.sa{tb}")
            DVE(lambda e, tb=tb: e.tensor_tensor(out=SS[:, 7:8], in0=SS[:, 6:7],
                                                 in1=STATS[:, tb * 8 + 1:tb * 8 + 2], op=ALU.mult),
                waits=[(s_dve, s_dve.count)], tag=f"x.sb{tb}")
            sb_at = s_dve.count
            ACT(lambda e: e.activation(out=SS[:, 8:9], in_=SS[:, 7:8], func=AF.Ln, bias=sCB[:, 1:2]),
                waits=[(s_dve, sb_at)], tag=f"x.l2{tb}")
            l2 = s_act.count
            ACT(lambda e, tb=tb: e.activation(out=STATS[:, tb * 8 + 2:tb * 8 + 3], in_=SS[:, 8:9],
                                              func=AF.Exp, scale=-0.5),
                waits=[(s_act, l2)], tag=f"x.st{tb}")
            DVE(lambda e, tb=tb: e.tensor_scalar_add(out=STATS[:, tb * 8 + 4:tb * 8 + 5],
                                                     in0=STATS[:, tb * 8 + 1:tb * 8 + 2], scalar1=1.0),
                waits=[(s_act, inv_at), (s_dve, s_dve.count)], tag=f"x.B{tb}")
            DVE(lambda e, tb=tb: e.tensor_tensor(out=STATS[:, tb * 8 + 3:tb * 8 + 4],
                                                 in0=STATS[:, tb * 8:tb * 8 + 1],
                                                 in1=STATS[:, tb * 8 + 1:tb * 8 + 2], op=ALU.mult),
                waits=[(s_dve, s_dve.count)], tag=f"x.C{tb}")
            DVE(lambda e, tb=tb, b=b: e.tensor_scalar(out=xnb[b][:], in0=xin[b][:],
                                                      scalar1=STATS[:, tb * 8:tb * 8 + 1],
                                                      scalar2=STATS[:, tb * 8 + 1:tb * 8 + 2],
                                                      op0=ALU.subtract, op1=ALU.mult),
                waits=[(s_pe, stx["xnb"][b]), (s_dve, s_dve.count)], tag=f"x.xn{tb}")
            stx["xin"][b] = s_dve.count
            xn_at = s_dve.count
            for half in range(2):
                bank = half
                w4 = [(s_dve, xn_at), (s_dve, bank_dve[bank]), (s_act, bank_act[bank])]
                for j in range(8):
                    ch = half * 8 + j
                    def tr(e, ch=ch, j=j, bank=bank, b=b):
                        return e.transpose(pbf(bank)[:, j * 128:(j + 1) * 128],
                                           xnb[b][:, ch * 128:(ch + 1) * 128], sI[:])
                    PE(tr, waits=(w4 if j == 0 else ()), inc=(j == 7), tag=f"x.tr{tb}.{ch}")
                tr_at = s_pe.count
                def ev(e, half=half, bank=bank, tb=tb):
                    src = pbf(bank)[:, 0:1024].rearrange("p (ch m) -> p ch m", ch=8)
                    dst = XF.rearrange("p (ch m) -> p ch m", ch=16)[
                        :, half * 8:half * 8 + 8, tb * 128:(tb + 1) * 128]
                    return e.tensor_copy(out=dst, in_=src)
                DVE(ev, waits=[(s_pe, tr_at)] + ([(w_T, PREP_ALL_WR)] if tb == 0 else []),
                    tag=f"x.ev{tb}.{half}")
                bank_dve[bank] = s_dve.count
            stx["xnb"][b] = s_pe.count
        RC.release()
        XF_DVE = s_dve.count
        TAP("XF", XF[:, :], [(s_dve, XF_DVE)], "XF")
        TAP("STATS", STATS[:, :], [(s_dve, s_dve.count), (s_act, s_act.count)], "STATS")

        RA.mark()
        qslab = bfb(RA, 4 * 6144)
        def qsl(tbl):
            return qslab[:, tbl * 6144:(tbl + 1) * 6144]
        wslab = [bfb(RA, 8192) for _ in range(2)]
        RC.mark()
        sigb = bfb(RC, 4 * 2048)
        wb = bfb(RC, 2048)
        wvb = bfb(RC, 2048)
        r1b = f32b(RC, 512)
        r2b = f32b(RC, 512)
        t1b = r1b
        yfsl = bfb(RC, 16 * 512)
        def ybt(tbl):
            return qsl(tbl)[:, 0:2048]

        st1 = dict(wsl=[0, 0], slab_act=0, slab_dve=0, yf_wr=0, psb_act=0,
                   wb_pe=0, wvb_pe=0, r2_prev=0, yb_pe=0)

        for s in range(4):
            tb0 = 4 * s
            for oi in range(12):
                wbuf = oi % 2
                w5 = [(s_pe, st1["wsl"][wbuf])]
                if s == 0 and oi < 2:
                    w5.append((w_T, PREP_ALL_WR))
                src = Tqkv[:, oi * 512:(oi + 1) * 512].rearrange("(ch p) m -> p ch m", p=128)
                dst = wslab[wbuf][:, 0:8192].rearrange("p (ch m) -> p ch m", ch=16)
                LD(lambda e, s2=src, d2=dst: e.dma_start(out=d2, in_=s2), waits=w5, tag=f"q.w{s}.{oi}",
                   sem=c_wq[wbuf])
                wld = c_wq[wbuf].count
                for tbl in range(4):
                    tb = tb0 + tbl
                    bank = (oi * 4 + tbl) % 6
                    w6 = [(c_wq[wbuf], wld), (s_act, bank_act[bank]), (s_dve, bank_dve[bank]),
                          (s_dve, XF_DVE)]
                    if s > 0 and oi == 0:
                        w6 += [(s_act, st1["slab_act"]), (s_dve, st1["slab_dve"])]
                    for ch in range(16):
                        def mm(e, ch=ch, tb=tb, wbuf=wbuf, bank=bank):
                            return e.matmul(pba(bank)[:, 0:512],
                                            XFt(ch)[:, tb * 128:(tb + 1) * 128],
                                            wslab[wbuf][:, ch * 512:(ch + 1) * 512],
                                            start=(ch == 0), stop=(ch == 15))
                        PE(mm, waits=(w6 if ch == 0 else ()), inc=(ch == 15), tag=f"q.mm{s}.{oi}.{tbl}")
                    mm_at = s_pe.count
                    ACT(lambda e, tbl=tbl, oi=oi, bank=bank, tb=tb: e.activation(
                            out=qsl(tbl)[:, oi * 512:(oi + 1) * 512], in_=pba(bank)[:, 0:512],
                            func=AF.Copy, scale=STATS[:, tb * 8 + 2:tb * 8 + 3]),
                        waits=[(s_pe, mm_at)], tag=f"q.ev{s}.{oi}.{tbl}")
                    bank_act[bank] = s_act.count
                st1["wsl"][wbuf] = s_pe.count
            qkv_act = s_act.count
            if s == 0:
                TAP("qkv0", qslab[:, :], [(s_act, qkv_act)], "qkv0")

            sq_sc = [0] * 4
            for tbl in range(4):
                tb = tb0 + tbl
                ACT(lambda e, tbl=tbl: e.activation(out=DUMP[:], in_=qsl(tbl)[:, 0:1024], func=AF.Square,
                                                    accum_out=SS[:, 20:21]),
                    waits=[(s_act, qkv_act)], tag=f"m.q2{s}.{tbl}a")
                ACT(lambda e, tbl=tbl: e.activation(out=DUMP[:], in_=qsl(tbl)[:, 1024:2048], func=AF.Square,
                                                    accum_out=SS[:, 21:22]),
                    waits=[(s_act, s_act.count)], tag=f"m.q2{s}.{tbl}b")
                ACT(lambda e, tbl=tbl: e.activation(out=DUMP[:], in_=qsl(tbl)[:, 2048:3072], func=AF.Square,
                                                    accum_out=SS[:, 22:23]),
                    waits=[(s_act, s_act.count)], tag=f"m.k2{s}.{tbl}a")
                ACT(lambda e, tbl=tbl: e.activation(out=DUMP[:], in_=qsl(tbl)[:, 3072:4096], func=AF.Square,
                                                    accum_out=SS[:, 23:24]),
                    waits=[(s_act, s_act.count)], tag=f"m.k2{s}.{tbl}b")
                a2 = s_act.count
                DVE(lambda e: e.tensor_tensor(out=SS[:, 24:25], in0=SS[:, 20:21], in1=SS[:, 21:22],
                                              op=ALU.add),
                    waits=[(s_act, a2)], tag=f"m.qs{s}.{tbl}")
                DVE(lambda e: e.tensor_tensor(out=SS[:, 25:26], in0=SS[:, 22:23], in1=SS[:, 23:24],
                                              op=ALU.add), waits=[(s_dve, s_dve.count)], tag=f"m.ks{s}.{tbl}")
                sum_at2 = s_dve.count
                ACT(lambda e: e.activation(out=SS[:, 26:27], in_=SS[:, 24:25], func=AF.Ln,
                                           scale=1.0 / 2048.0, bias=sCB[:, 1:2]),
                    waits=[(s_dve, sum_at2)], tag=f"m.lq{s}.{tbl}")
                a3 = s_act.count
                ACT(lambda e: e.activation(out=SS[:, 27:28], in_=SS[:, 25:26], func=AF.Ln,
                                           scale=1.0 / 2048.0, bias=sCB[:, 1:2]),
                    waits=[(s_act, a3)], tag=f"m.lk{s}.{tbl}")
                a4 = s_act.count
                ACT(lambda e, tb=tb: e.activation(out=STATS[:, tb * 8 + 5:tb * 8 + 6], in_=SS[:, 26:27],
                                                  func=AF.Exp, scale=-0.5),
                    waits=[(s_act, a3)], tag=f"m.sq{s}.{tbl}")
                ACT(lambda e, tb=tb: e.activation(out=STATS[:, tb * 8 + 6:tb * 8 + 7], in_=SS[:, 27:28],
                                                  func=AF.Exp, scale=-0.5),
                    waits=[(s_act, a4)], tag=f"m.sk{s}.{tbl}")
                sq_sc[tbl] = s_act.count

            for tbl in range(4):
                tb = tb0 + tbl
                ACT(lambda e, tbl=tbl, tb=tb: e.activation(
                        out=sigb[:, tbl * 2048:(tbl + 1) * 2048], in_=qsl(tbl)[:, 0:2048],
                        func=AF.Sigmoid, scale=STATS[:, tb * 8 + 5:tb * 8 + 6]),
                    waits=[(s_act, sq_sc[tbl]), (s_dve, st1["slab_dve"])],
                    tag=f"m.sig{s}.{tbl}")
            sig_at = s_act.count

            for tbl in range(4):
                tb = tb0 + tbl
                ACT(lambda e, tbl=tbl, tb=tb: e.activation(out=wb[:], in_=qsl(tbl)[:, 2048:4096],
                                                           func=AF.Exp,
                                                           scale=STATS[:, tb * 8 + 6:tb * 8 + 7]),
                    waits=[(s_act, sig_at), (s_pe, st1["wb_pe"])], tag=f"m.w{s}.{tbl}")
                w_at = s_act.count
                DVE(lambda e, tbl=tbl: e.tensor_tensor(out=wvb[:], in0=wb[:], in1=qsl(tbl)[:, 4096:6144],
                                                       op=ALU.mult),
                    waits=[(s_act, w_at), (s_pe, st1["wvb_pe"])], tag=f"m.wv{s}.{tbl}")
                wv_at = s_dve.count
                r2_ats = [0] * 4
                t1_ats = [0] * 4
                for q in range(4):
                    for typ in range(2):
                        bank = q * 2 + typ
                        msrc = wb if typ == 0 else wvb
                        w7 = [(s_dve, wv_at)] if typ == 1 else [(s_act, w_at)]
                        if tbl == 0:
                            w7 += [(s_act, bank_act[bank]), (s_dve, bank_dve[bank])]
                        carry = (tbl == 0 and s > 0)
                        PE(lambda e, q=q, bank=bank, m=msrc, first=(tbl == 0): e.matmul(
                                pba(bank)[:, 0:512], sTRIU[:], m[:, q * 512:(q + 1) * 512],
                                start=first, stop=False),
                           waits=w7, inc=(not carry), tag=f"m.tri{s}.{tbl}.{q}.{typ}")
                        if carry:
                            PE(lambda e, q=q, bank=bank, typ=typ: e.matmul(
                                    pba(bank)[:, 0:512], sFON[0:1, :],
                                    PSB[0:1, typ * D + q * 512: typ * D + (q + 1) * 512],
                                    start=False, stop=False),
                               waits=[(s_act, st1["psb_act"])], inc=True, tag=f"m.car{s}.{q}.{typ}")
                    cum_at = s_pe.count
                    ACT(lambda e, q=q: e.activation(out=r1b[:], in_=pba(q * 2)[:, 0:512], func=AF.Ln,
                                                    bias=sCB[:, 2:3]),
                        waits=[(s_pe, cum_at), (s_act, st1["r2_prev"]), (s_dve, st1.get("t1rd", 0))],
                        tag=f"m.r1{s}.{tbl}.{q}")
                    r1_at = s_act.count
                    ACT(lambda e: e.activation(out=r2b[:], in_=r1b[:], func=AF.Exp, scale=-1.0),
                        waits=[(s_act, r1_at)], tag=f"m.r2{s}.{tbl}.{q}")
                    r2_ats[q] = s_act.count
                    st1["r2_prev"] = r2_ats[q]
                    DVE(lambda e, q=q: e.tensor_tensor(out=t1b[:], in0=pba(q * 2 + 1)[:, 0:512],
                                                       in1=r2b[:], op=ALU.mult),
                        waits=[(s_act, r2_ats[q])], tag=f"m.t1{s}.{tbl}.{q}")
                    t1_ats[q] = s_dve.count
                    DVE(lambda e, q=q, tbl=tbl: e.tensor_tensor(
                            out=ybt(tbl)[:, q * 512:(q + 1) * 512],
                            in0=t1b[:],
                            in1=sigb[:, tbl * 2048 + q * 512: tbl * 2048 + (q + 1) * 512],
                            op=ALU.mult),
                        waits=[(s_act, sig_at), (s_pe, st1["yb_pe"]), (s_dve, t1_ats[q])],
                        tag=f"m.y{s}.{tbl}.{q}")
                    st1["t1rd"] = s_dve.count
                for q in range(4):
                    for typ in range(2):
                        bank = q * 2 + typ
                        msrc = wb if typ == 0 else wvb
                        PE(lambda e, bank=bank, m=msrc, q=q, last=(tbl == 3): e.matmul(
                                pba(bank)[:, 0:512], sSTL[:], m[:, q * 512:(q + 1) * 512],
                                start=False, stop=last),
                           waits=[(s_act, r2_ats[q])] if typ == 0 else [(s_dve, t1_ats[q])],
                           inc=True, tag=f"m.stl{s}.{tbl}.{q}.{typ}")
                st1["wb_pe"] = s_pe.count
                st1["wvb_pe"] = s_pe.count
            stl_all = s_pe.count
            for q in range(4):
                for typ in range(2):
                    bank = q * 2 + typ
                    ACT(lambda e, q=q, typ=typ, bank=bank: e.copy(
                            out=PSB[0:1, typ * D + q * 512: typ * D + (q + 1) * 512],
                            in_=pba(bank)[0:1, 0:512]),
                        waits=[(s_pe, stl_all)], tag=f"m.pev{s}.{q}.{typ}")
                    bank_act[bank] = s_act.count
            st1["psb_act"] = s_act.count
            st1["slab_act"] = s_act.count
            if s == 0:
                TAP("y0", ybt(0)[:], [(s_dve, s_dve.count)], "y0")

            y_all = s_dve.count
            for tbl in range(4):
                for half in range(2):
                    hb = (tbl * 2 + half) % 2
                    w9 = [(s_dve, y_all), (s_dve, bank_dve[hb]), (s_act, bank_act[hb]),
                          (w_yfs, st1["yf_wr"])]
                    for j in range(8):
                        ch = half * 8 + j
                        def tr2(e, ch=ch, j=j, hb=hb, tbl=tbl):
                            return e.transpose(pbf(hb)[:, j * 128:(j + 1) * 128],
                                               ybt(tbl)[:, ch * 128:(ch + 1) * 128],
                                               sI[:])
                        PE(tr2, waits=(w9 if j == 0 else ()), inc=(j == 7), tag=f"y.tr{s}.{tbl}.{half}")
                    tr_at = s_pe.count
                    def ev(e, half=half, hb=hb, tbl=tbl):
                        src = pbf(hb)[:, 0:1024].rearrange("p (ch m) -> p ch m", ch=8)
                        dst = yfsl.rearrange("p (ch m) -> p ch m", ch=16)[
                            :, half * 8:half * 8 + 8, tbl * 128:(tbl + 1) * 128]
                        return e.tensor_copy(out=dst, in_=src)
                    DVE(ev, waits=[(s_pe, tr_at)], tag=f"y.ev{s}.{tbl}.{half}")
                    bank_dve[hb] = s_dve.count
            st1["yb_pe"] = s_pe.count
            yf_dve = s_dve.count
            st1["slab_dve"] = s_dve.count
            dstw = YFS[:, s * 512:(s + 1) * 512].rearrange("(ch p) m -> p ch m", p=128)
            srcw = yfsl[:, 0:16 * 512].rearrange("p (ch m) -> p ch m", ch=16)
            WR(lambda e, s2=srcw, d2=dstw: e.dma_start(out=d2, in_=s2),
               waits=[(s_dve, yf_dve)], tag=f"y.wr{s}", sem=w_yfs)
            st1["yf_wr"] = w_yfs.count
        RC.release()
        RA.release()
        P1_WR = w_yfs.count
        TAP("YFSfull", YFS[:, :], [(w_yfs, P1_WR)], "YFSfull")

        RA.mark()
        RC.mark()
        slot_bufs = [bfb(RA, 2048) for _ in range(16)] + [bfb(RC, 2048) for _ in range(8)]
        sSUH = bfb(RC, 16 * 128)
        slotmap = list(range(16))
        free_slots = [16, 17, 18, 19, 20, 21, 22, 23]
        for a in range(16):
            DVE(lambda e, a=a: e.tensor_scalar_mul(out=sSUH[:, a * 128:(a + 1) * 128], in0=sH[:],
                                                   scalar1=sSUu[:, a:a + 1]),
                waits=[(c_const, C_LD), (s_pe, s_pe.count), (w_yfs, P1_WR)], tag=f"c.suH{a}")
        SUH_DVE = s_dve.count
        TAP("suH", sSUH[:, :], [(s_dve, SUH_DVE)], "suH")

        for a in range(16):
            for tq in range(4):
                bank = (a * 4 + tq) % 8
                w10 = [(s_act, bank_act[bank]), (s_dve, bank_dve[bank]), (s_dve, XF_DVE),
                       (s_dve, SUH_DVE)]
                PE(lambda e, a=a, tq=tq, bank=bank: e.matmul(
                        pba(bank)[:, 0:512], sSUH[:, a * 128:(a + 1) * 128],
                        XFt(a)[:, tq * 512:(tq + 1) * 512], start=True, stop=True),
                   waits=w10, inc=True, tag=f"f1.mm{a}.{tq}")
                mm_at = s_pe.count
                ACT(lambda e, a=a, tq=tq, bank=bank: e.copy(
                        out=slot_bufs[a][:, tq * 512:(tq + 1) * 512], in_=pba(bank)[:, 0:512]),
                    waits=[(s_pe, mm_at)], tag=f"f1.ev{a}.{tq}")
                bank_act[bank] = s_act.count
        f1_act = s_act.count

        first_round = True
        for h in (1, 2, 4, 8):
            for a0 in range(16):
                if a0 & h:
                    continue
                a1 = a0 + h
                sA, sB = slotmap[a0], slotmap[a1]
                dP = free_slots.pop(0)
                dM = free_slots.pop(0)
                wts = [(s_act, f1_act)] if first_round else []
                DVE(lambda e, sA=sA, sB=sB, dP=dP: e.tensor_tensor(
                        out=slot_bufs[dP][:], in0=slot_bufs[sA][:], in1=slot_bufs[sB][:], op=ALU.add),
                    waits=wts + [(s_dve, s_dve.count - 5)], tag=f"f1.bp{h}.{a0}")
                DVE(lambda e, sA=sA, sB=sB, dM=dM: e.tensor_tensor(
                        out=slot_bufs[dM][:], in0=slot_bufs[sA][:], in1=slot_bufs[sB][:], op=ALU.subtract),
                    waits=[(s_dve, s_dve.count - 5)], tag=f"f1.bm{h}.{a0}")
                slotmap[a0], slotmap[a1] = dP, dM
                free_slots.append(sA)
                free_slots.append(sB)
            first_round = False
        X1_DVE = s_dve.count
        x1map = list(slotmap)
        def X1t(ch):
            return slot_bufs[x1map[ch]]
        TAP("x1t0", X1t(0)[:], [(s_dve, X1_DVE)], "x1t0")
        TAP("x1t5", X1t(5)[:], [(s_dve, X1_DVE)], "x1t5")

        if stop_after == "P2":
            P.emit("gpsimd", None, waits=[(s_wr, s_wr.count)], inc=None, tag="final")
            P.check()
            with nc.Block() as block:
                def replay(engname):
                    def go(eng):
                        for op in P.ops[engname]:
                            for sem, v in op.waits:
                                eng.wait_ge(sem.h, v)
                            inst = op.fn(eng) if op.fn is not None else None
                            for sem, n in op.incs:
                                inst.then_inc(sem.h, n)
                    return go
                block.sync(replay("sync"))
                block.vector(replay("vector"))
                block.scalar(replay("scalar"))
                block.tensor(replay("tensor"))
                block.gpsimd(replay("gpsimd"))
            return nc

        RB.release()
        RB.mark()
        upst = [bfb(RB, 8192) for _ in range(2)]
        gst = [bfb(RB, 8192) for _ in range(2)]
        gbuf3 = [bfb(RC, 512) for _ in range(2)]
        x2acc = [bfb(RC, 2048) for _ in range(2)]
        p2_pe = s_pe.count
        st3 = dict(up_pe=[p2_pe, p2_pe], g_pe=[p2_pe, p2_pe], gb_dve=[0, 0], x2_wr=[0, 0])

        for fg in range(16):
            bsel = fg % 2
            w11 = [(s_pe, st3["up_pe"][bsel])]
            if fg < 2:
                w11.append((w_T, PREP_UG_WR))
            src = Tup[:, fg * 512:(fg + 1) * 512].rearrange("(ch p) m -> p ch m", p=128)
            dst = upst[bsel][:, 0:8192].rearrange("p (ch m) -> p ch m", ch=16)
            LD(lambda e, s2=src, d2=dst: e.dma_start(out=d2, in_=s2), waits=w11, tag=f"u.wu{fg}",
               sem=c_up[bsel])
            src2 = Tgate[:, fg * 512:(fg + 1) * 512].rearrange("(ch p) m -> p ch m", p=128)
            dst2 = gst[bsel][:, 0:8192].rearrange("p (ch m) -> p ch m", ch=16)
            LD(lambda e, s2=src2, d2=dst2: e.dma_start(out=d2, in_=s2),
               waits=[(s_pe, st3["g_pe"][bsel])] + ([(w_T, PREP_UG_WR)] if fg < 2 else []),
               tag=f"u.wg{fg}", sem=c_gp[bsel])
            wld_u = c_up[bsel].count
            wld_g = c_gp[bsel].count
            for fl in range(4):
                f = fg * 4 + fl
                xa = f % 2
                for tq in range(4):
                    bu = tq
                    bg = 4 + tq
                    w12 = [(c_up[bsel], wld_u), (s_act, bank_act[bu]), (s_dve, bank_dve[bu]),
                           (s_dve, X1_DVE)]
                    for ch in range(16):
                        def mmu(e, ch=ch, fl=fl, tq=tq, bu=bu, bsel=bsel):
                            return e.matmul(pba(bu)[:, 0:512],
                                            upst[bsel][:, ch * 512 + fl * 128: ch * 512 + (fl + 1) * 128],
                                            X1t(ch)[:, tq * 512:(tq + 1) * 512],
                                            start=(ch == 0), stop=(ch == 15))
                        PE(mmu, waits=(w12 if ch == 0 else ()), inc=(ch == 15), tag=f"u.mmu{f}.{tq}")
                    w13 = [(c_gp[bsel], wld_g), (s_act, bank_act[bg]), (s_dve, bank_dve[bg])]
                    for ch in range(16):
                        def mmg(e, ch=ch, fl=fl, tq=tq, bg=bg, bsel=bsel):
                            return e.matmul(pba(bg)[:, 0:512],
                                            gst[bsel][:, ch * 512 + fl * 128: ch * 512 + (fl + 1) * 128],
                                            X1t(ch)[:, tq * 512:(tq + 1) * 512],
                                            start=(ch == 0), stop=(ch == 15))
                        PE(mmg, waits=(w13 if ch == 0 else ()), inc=(ch == 15), tag=f"u.mmg{f}.{tq}")
                    mmg_at = s_pe.count
                    gb = tq % 2
                    ACT(lambda e, bg=bg, gb=gb: e.activation(out=gbuf3[gb][:], in_=pba(bg)[:, 0:512],
                                                             func=AF.Silu),
                        waits=[(s_pe, mmg_at), (s_dve, st3["gb_dve"][gb])], tag=f"u.sil{f}.{tq}")
                    sil_at = s_act.count
                    bank_act[bg] = sil_at
                    DVE(lambda e, bu=bu, gb=gb, f=f, tq=tq, xa=xa: e.scalar_tensor_tensor(
                            out=x2acc[xa][:, tq * 512:(tq + 1) * 512], in0=pba(bu)[:, 0:512],
                            scalar=sSUd[:, f:f + 1], in1=gbuf3[gb][:], op0=ALU.mult, op1=ALU.mult),
                        waits=[(s_act, sil_at), (w_x2s, st3["x2_wr"][xa])], tag=f"u.x2{f}.{tq}")
                    bank_dve[bu] = s_dve.count
                    st3["gb_dve"][gb] = s_dve.count
                x2_at = s_dve.count
                WR(lambda e, f=f, xa=xa: e.dma_start(out=X2S[f * 128:(f + 1) * 128, :], in_=x2acc[xa][:]),
                   waits=[(s_dve, x2_at)], tag=f"u.wr{f}", sem=w_x2s)
                st3["x2_wr"][xa] = w_x2s.count
            st3["up_pe"][bsel] = s_pe.count
            st3["g_pe"][bsel] = s_pe.count
        RB.release()
        P3_WR = w_x2s.count
        RA.release()
        RC.release()

        if stop_after == "P3":
            P.emit("gpsimd", None, waits=[(s_wr, s_wr.count)], inc=None, tag="final")
            P.check()
            with nc.Block() as block:
                def replay(engname):
                    def go(eng):
                        for op in P.ops[engname]:
                            for sem, v in op.waits:
                                eng.wait_ge(sem.h, v)
                            inst = op.fn(eng) if op.fn is not None else None
                            for sem, n in op.incs:
                                inst.then_inc(sem.h, n)
                    return go
                block.sync(replay("sync"))
                block.vector(replay("vector"))
                block.scalar(replay("scalar"))
                block.tensor(replay("tensor"))
                block.gpsimd(replay("gpsimd"))
            return nc

        p3_pe = s_pe.count
        st4 = dict(ust_pe=[p3_pe, p3_pe], gst_pe=[p3_pe, p3_pe], gb_dve=[0, 0], z_pe=p3_pe,
                   seg_pe=[0, 0], asw_wr=[0, 0], yf_used_pe=p3_pe, astg_wr=0)

        def zt(f):
            return AR[:, f * 1024:(f + 1) * 1024]

        for s in range(2):
            RC.mark()
            yfs = raw_bf(65536, 16 * 1024)
            RC.alloc(81920 - 73728)
            ust4 = [bfb(RC, 4096) for _ in range(2)]
            gst4 = [bfb(RC, 4096) for _ in range(2)]
            gb4 = [bfb(RC, 512) for _ in range(2)]
            w14 = [(w_yfs, P1_WR), (w_asw, st4["astg_wr"]), (s_pe, st4["yf_used_pe"])]
            src = YFS[:, s * 1024:(s + 1) * 1024].rearrange("(ch p) m -> p ch m", p=128)
            dst = yfs[:, 0:16 * 1024].rearrange("p (ch m) -> p ch m", ch=16)
            LD(lambda e, s2=src, d2=dst: e.dma_start(out=d2, in_=s2), waits=w14, tag=f"v.yf{s}",
               sem=c_yf)
            yf_ld = c_yf.count
            for fg in range(32):
                bsel = fg % 2
                w15 = [(s_pe, st4["ust_pe"][bsel])]
                if fg < 2:
                    w15 += [(w_T, PREP_SWO_WR), (w_asw, st4["astg_wr"])]
                srcu = Tsw[:, fg * 256:(fg + 1) * 256].rearrange("(ch p) m -> p ch m", p=128)
                dstu = ust4[bsel][:, 0:4096].rearrange("p (ch m) -> p ch m", ch=16)
                LD(lambda e, s2=srcu, d2=dstu: e.dma_start(out=d2, in_=s2), waits=w15, tag=f"v.wu{s}.{fg}",
                   sem=c_us[bsel])
                srcg = Tsw[:, DFF + fg * 256: DFF + (fg + 1) * 256].rearrange("(ch p) m -> p ch m", p=128)
                dstg = gst4[bsel][:, 0:4096].rearrange("p (ch m) -> p ch m", ch=16)
                LD(lambda e, s2=srcg, d2=dstg: e.dma_start(out=d2, in_=s2),
                   waits=[(s_pe, st4["gst_pe"][bsel])] +
                         ([(w_T, PREP_SWO_WR), (w_asw, st4["astg_wr"])] if fg < 2 else []),
                   tag=f"v.wg{s}.{fg}", sem=c_gs[bsel])
                wld_u = c_us[bsel].count
                wld_g = c_gs[bsel].count
                for fl in range(2):
                    f = fg * 2 + fl
                    for th in range(2):
                        bu = th * 2
                        bg = th * 2 + 1
                        w16 = [(c_us[bsel], wld_u), (c_yf, yf_ld), (s_act, bank_act[bu]),
                               (s_dve, bank_dve[bu])]
                        for ch in range(16):
                            def mmu(e, bu=bu, ch=ch,
                                    st_=ust4[bsel][:, ch * 256 + fl * 128: ch * 256 + (fl + 1) * 128],
                                    mv=yfs[:, ch * 1024 + th * 512: ch * 1024 + (th + 1) * 512]):
                                return e.matmul(pba(bu)[:, 0:512], st_, mv,
                                                start=(ch == 0), stop=(ch == 15))
                            PE(mmu, waits=(w16 if ch == 0 else ()), inc=(ch == 15), tag=f"v.mmu{s}.{f}.{th}")
                        w17 = [(c_gs[bsel], wld_g), (s_act, bank_act[bg]), (s_dve, bank_dve[bg])]
                        for ch in range(16):
                            def mmg(e, bg=bg, ch=ch,
                                    st_=gst4[bsel][:, ch * 256 + fl * 128: ch * 256 + (fl + 1) * 128],
                                    mv=yfs[:, ch * 1024 + th * 512: ch * 1024 + (th + 1) * 512]):
                                return e.matmul(pba(bg)[:, 0:512], st_, mv,
                                                start=(ch == 0), stop=(ch == 15))
                            PE(mmg, waits=(w17 if ch == 0 else ()), inc=(ch == 15), tag=f"v.mmg{s}.{f}.{th}")
                        mmg_at = s_pe.count
                        gb = th
                        ACT(lambda e, bg=bg, o2=gb4[gb][:]: e.activation(out=o2, in_=pba(bg)[:, 0:512],
                                                                 func=AF.Silu),
                            waits=[(s_pe, mmg_at), (s_dve, st4["gb_dve"][gb])], tag=f"v.sil{s}.{f}.{th}")
                        sil_at = s_act.count
                        bank_act[bg] = sil_at
                        DVE(lambda e, bu=bu, f=f, th=th, i1=gb4[gb][:]: e.tensor_tensor(
                                out=zt(f)[:, th * 512:(th + 1) * 512], in0=pba(bu)[:, 0:512],
                                in1=i1, op=ALU.mult),
                            waits=[(s_act, sil_at), (s_pe, st4["z_pe"])], tag=f"v.z{s}.{f}.{th}")
                        bank_dve[bu] = s_dve.count
                        st4["gb_dve"][gb] = s_dve.count
                st4["ust_pe"][bsel] = s_pe.count
                st4["gst_pe"][bsel] = s_pe.count
            z_dve = s_dve.count
            st4["yf_used_pe"] = s_pe.count
            RC.release()
            if s == 0:
                TAP("z0", AR[:, 0:8 * 1024], [(s_dve, z_dve)], "z0")
                TAP("zfull", AR[:, 0:64 * 1024], [(s_dve, z_dve)], "zfull")

            RC.mark()
            segs = [bfb(RC, 4096) for _ in range(2)]
            astg = [f32b(RC, 4096) for _ in range(2)]
            uv_pe = s_pe.count
            st4["seg_pe"] = [max(st4["seg_pe"][0], uv_pe), max(st4["seg_pe"][1], uv_pe)]
            for dq in range(4):
                ab = dq % 2
                for fseg in range(8):
                    bsel = fseg % 2
                    w18 = [(s_pe, st4["seg_pe"][bsel])]
                    if s == 0 and dq == 0 and fseg < 2:
                        w18.append((w_T, PREP_SWO_WR))
                    srcs = Twout[fseg * 1024:(fseg + 1) * 1024, dq * 512:(dq + 1) * 512].rearrange(
                        "(fh p) m -> p fh m", p=128)
                    dsts = segs[bsel][:, 0:4096].rearrange("p (fh m) -> p fh m", fh=8)
                    LD(lambda e, s2=srcs, d2=dsts: e.dma_start(out=d2, in_=s2), waits=w18,
                       tag=f"o.seg{s}.{dq}.{fseg}", sem=c_sg4[bsel])
                    sld = c_sg4[bsel].count
                    for tbl in range(8):
                        bank = tbl
                        w19 = [(c_sg4[bsel], sld), (s_dve, z_dve)]
                        if fseg == 0:
                            w19 += [(s_act, bank_act[bank]), (s_dve, bank_dve[bank])]
                        for fl in range(8):
                            f = fseg * 8 + fl
                            def mmo(e, f=f, fl=fl, tbl=tbl, bank=bank, fseg=fseg,
                                    mv=segs[bsel][:, fl * 512:(fl + 1) * 512]):
                                return e.matmul(pba(bank)[:, 0:512],
                                                zt(f)[:, tbl * 128:(tbl + 1) * 128],
                                                mv,
                                                start=(fseg == 0 and fl == 0), stop=(fseg == 7 and fl == 7))
                            PE(mmo, waits=(w19 if fl == 0 else ()), inc=(fl == 7),
                               tag=f"o.mm{s}.{dq}.{fseg}.{tbl}")
                    st4["seg_pe"][bsel] = s_pe.count
                mm_at = s_pe.count
                if s == 1 and dq == 0:
                    dbgW = bfb(RC, 1024)
                    ACT(lambda e, d=dbgW: e.copy(out=d.bitcast(F32)[:, 0:512], in_=pba(7)[:, 0:512]),
                        waits=[(s_pe, mm_at)], tag="o.dbgpsum")
                    TAP("wodbg", dbgW.bitcast(F32)[:, 0:512], [(s_act, s_act.count)], "wodbg")
                    TAP("segdbg", segs[1][:, 0:1024], [(s_pe, mm_at)], "segdbg")
                for tbl in range(8):
                    ACT(lambda e, tbl=tbl, o2=astg[ab][:, tbl * 512:(tbl + 1) * 512]: e.copy(
                            out=o2, in_=pba(tbl)[:, 0:512]),
                        waits=([(s_pe, mm_at), (w_asw, st4["asw_wr"][ab])] if tbl == 0
                               else [(s_pe, mm_at)]),
                        tag=f"o.ev{s}.{dq}.{tbl}")
                    bank_act[tbl] = s_act.count
                    ev_at = s_act.count
                    WR(lambda e, s2=astg[ab][:, tbl * 512:(tbl + 1) * 512],
                             d2=ASW[s * 1024 + tbl * 128: s * 1024 + (tbl + 1) * 128,
                                    dq * 512:(dq + 1) * 512]: e.dma_start(out=d2, in_=s2),
                       waits=[(s_act, ev_at)], tag=f"o.wr{s}.{dq}.{tbl}", sem=w_asw)
                st4["asw_wr"][ab] = w_asw.count
            st4["z_pe"] = s_pe.count
            st4["yf_used_pe"] = s_pe.count
            st4["astg_wr"] = w_asw.count
            RC.release()
        P4_WR = s_wr.count

        if stop_after == "P4":
            P.emit("gpsimd", None, waits=[(s_wr, s_wr.count)], inc=None, tag="final")
            P.check()
            with nc.Block() as block:
                def replay(engname):
                    def go(eng):
                        for op in P.ops[engname]:
                            for sem, v in op.waits:
                                eng.wait_ge(sem.h, v)
                            inst = op.fn(eng) if op.fn is not None else None
                            for sem, n in op.incs:
                                inst.then_inc(sem.h, n)
                    return go
                block.sync(replay("sync"))
                block.vector(replay("vector"))
                block.scalar(replay("scalar"))
                block.tensor(replay("tensor"))
                block.gpsimd(replay("gpsimd"))
            return nc

        st5 = dict(seg_pe=[0, 0], x2_pe=s_pe.count, out_wr=[0, 0], xa_dve=[0, 0], aswa_dve=[0, 0],
                   x2_dve=s_dve.count, x2_pool=0)
        for s in range(4):
            RA.mark()
            RB.mark()
            RC.mark()
            x2sl = [bfb(RA, 512) for _ in range(64)] + [bfb(RA, 512) for _ in range(14)]
            x2map = list(range(64))
            dve_free = [64, 65, 66, 67, 68, 69, 70, 71]
            pool_free = [72, 73, 74, 75, 76, 77]
            def sbuf_of(i, x2sl=x2sl):
                return x2sl[i]
            r0a4 = [f32b(RB, 2048) for _ in range(4)]
            segs = [bfb(RB, 4096) for _ in range(2)]
            xsl = [f32b(RC, 2048) for _ in range(2)]
            aswsl = [f32b(RC, 2048) for _ in range(2)]
            r0t = f32b(RC, 2048)
            ostg = [f32b(RC, 512) for _ in range(2)]

            for fseg in range(8):
                w20 = [(w_x2s, P3_WR), (s_pe, st5["x2_pe"]), (s_dve, st5["x2_dve"]),
                       (s_pool, st5["x2_pool"])]
                src = X2S[fseg * 1024:(fseg + 1) * 1024, s * 512:(s + 1) * 512].rearrange(
                    "(fh p) m -> p fh m", p=128)
                dst = AR[:, fseg * 8 * 512:(fseg + 1) * 8 * 512].rearrange("p (fh m) -> p fh m", fh=8)
                LD(lambda e, s2=src, d2=dst: e.dma_start(out=d2, in_=s2), waits=w20, tag=f"d.x2{s}.{fseg}",
                   sem=c_x2)
            x2_ld = c_x2.count

            for a in range(64):
                bank = a % 8
                w21 = [(c_x2, x2_ld), (s_act, bank_act[bank]), (s_dve, bank_dve[bank])]
                PE(lambda e, bank=bank, mv=x2sl[a][:]: e.matmul(pba(bank)[:, 0:512], sH[:],
                                                      mv, start=True, stop=True),
                   waits=w21, inc=True, tag=f"d.fA{s}.{a}")
                mm_at = s_pe.count
                ACT(lambda e, bank=bank, o2=x2sl[a][:]: e.copy(out=o2, in_=pba(bank)[:, 0:512]),
                    waits=[(s_pe, mm_at)], tag=f"d.fAe{s}.{a}")
                bank_act[bank] = s_act.count
            fA_act = s_act.count

            first = True
            rb_dve = 0
            rb_pool = 0
            for h in (1, 2, 4, 8, 16, 32):
                pairs = [a0 for a0 in range(64) if not (a0 & h)]
                n_dve = 24
                for pi, a0 in enumerate(pairs):
                    a1 = a0 + h
                    sA, sB = x2map[a0], x2map[a1]
                    on_dve = pi < n_dve
                    eng = DVE if on_dve else POOL
                    flist = dve_free if on_dve else pool_free
                    dP = flist.pop(0)
                    dM = flist.pop(0)
                    if first:
                        wts = [(s_act, fA_act)]
                    else:
                        wts = [(s_dve, rb_dve), (s_pool, rb_pool)]
                    if on_dve:
                        wts = wts + [(s_dve, s_dve.count - 5)]
                    else:
                        wts = wts + [(s_pool, s_pool.count - 3)]
                    eng(lambda e, sA=sA, sB=sB, dP=dP: e.tensor_tensor(
                            out=sbuf_of(dP)[:], in0=sbuf_of(sA)[:], in1=sbuf_of(sB)[:], op=ALU.add),
                        waits=wts, tag=f"d.bp{s}.{h}.{a0}")
                    eng(lambda e, sA=sA, sB=sB, dM=dM: e.tensor_tensor(
                            out=sbuf_of(dM)[:], in0=sbuf_of(sA)[:], in1=sbuf_of(sB)[:], op=ALU.subtract),
                        waits=[(s_dve, s_dve.count - 5)] if on_dve else [(s_pool, s_pool.count - 3)],
                        tag=f"d.bm{s}.{h}.{a0}")
                    x2map[a0], x2map[a1] = dP, dM
                    flist.append(sA)
                    flist.append(sB)
                rb_dve = s_dve.count
                rb_pool = s_pool.count
                first = False
            x3_dve = s_dve.count
            x3_pool = s_pool.count
            def X3t(f, x2map=x2map, sbuf_of=sbuf_of):
                return sbuf_of(x2map[f])
            if s == 0:
                TAP("x3t0", X3t(0)[:], [(s_dve, x3_dve), (s_pool, x3_pool)], "x3t0")
            if s == 3:
                TAP("x3s3", X3t(0)[:], [(s_dve, x3_dve), (s_pool, x3_pool)], "x3s3")
                TAP("x2s3", x2sl[0][:] if False else AR[:, 0:512], [(c_x2, x2_ld)], "x2s3")

            for tbl in range(4):
                tb = s * 4 + tbl
                xa = tbl % 2
                LD(lambda e, tb=tb, o2=xsl[xa][:]: e.dma_start(out=o2, in_=x_in[tb * 128:(tb + 1) * 128, :]),
                   waits=[(s_dve, st5["xa_dve"][xa]), (w_asw, P4_WR)], tag=f"d.x{s}.{tbl}",
                   sem=c_xs[xa])
                LD(lambda e, tb=tb, o2=aswsl[xa][:]: e.dma_start(out=o2, in_=ASW[tb * 128:(tb + 1) * 128, :]),
                   waits=[(w_asw, P4_WR), (s_dve, st5["aswa_dve"][xa])], tag=f"d.a{s}.{tbl}",
                   sem=c_as[xa])
                res_ld_x = c_xs[xa].count
                res_ld_a = c_as[xa].count
                DVE(lambda e, tb=tb, i0=xsl[xa][:]: e.tensor_scalar(out=r0t[:], in0=i0,
                                                            scalar1=STATS[:, tb * 8 + 4:tb * 8 + 5],
                                                            scalar2=STATS[:, tb * 8 + 3:tb * 8 + 4],
                                                            op0=ALU.mult, op1=ALU.subtract),
                    waits=[(c_xs[xa], res_ld_x), (c_as[xa], res_ld_a)], tag=f"d.r0{s}.{tbl}")
                DVE(lambda e, o2=r0a4[tbl][:], i1=aswsl[xa][:]: e.tensor_tensor(out=o2, in0=r0t[:],
                                                              in1=i1, op=ALU.add),
                    waits=[(s_dve, s_dve.count)], tag=f"d.r0a{s}.{tbl}")
                st5["xa_dve"][xa] = s_dve.count
                st5["aswa_dve"][xa] = s_dve.count
            r0a_dve = s_dve.count
            if s == 0:
                TAP("r0a0", r0a4[0][:], [(s_dve, r0a_dve)], "r0a0")

            for dq in range(4):
                for fseg in range(8):
                    bsel = fseg % 2
                    w22 = [(s_pe, st5["seg_pe"][bsel])]
                    if s == 0 and dq == 0 and fseg < 2:
                        w22.append((w_T, PREP_ALL_WR))
                    srcs = Tdown[fseg * 1024:(fseg + 1) * 1024, dq * 512:(dq + 1) * 512].rearrange(
                        "(fh p) m -> p fh m", p=128)
                    dsts = segs[bsel][:, 0:4096].rearrange("p (fh m) -> p fh m", fh=8)
                    LD(lambda e, s2=srcs, d2=dsts: e.dma_start(out=d2, in_=s2), waits=w22,
                       tag=f"d.seg{s}.{dq}.{fseg}", sem=c_sg5[bsel])
                    sld = c_sg5[bsel].count
                    for tbl in range(4):
                        bank = 4 * (dq % 2) + tbl
                        w23 = [(c_sg5[bsel], sld), (s_dve, x3_dve), (s_pool, x3_pool)]
                        if fseg == 0:
                            w23 += [(s_act, bank_act[bank]), (s_dve, bank_dve[bank])]
                        for fl in range(8):
                            f = fseg * 8 + fl
                            def mmd(e, bank=bank, fseg=fseg, fl=fl,
                                    st_=X3t(f)[:, tbl * 128:(tbl + 1) * 128],
                                    mv=segs[bsel][:, fl * 512:(fl + 1) * 512]):
                                return e.matmul(pba(bank)[:, 0:512], st_, mv,
                                                start=(fseg == 0 and fl == 0), stop=(fseg == 7 and fl == 7))
                            PE(mmd, waits=(w23 if fl == 0 else ()), inc=(fl == 7),
                               tag=f"d.mm{s}.{dq}.{fseg}.{tbl}")
                    st5["seg_pe"][bsel] = s_pe.count
                dn_at = s_pe.count
                for tbl in range(4):
                    tb = s * 4 + tbl
                    bank = 4 * (dq % 2) + tbl
                    ob = (dq * 4 + tbl) % 2
                    DVE(lambda e, bank=bank, o2=ostg[ob][:],
                             i1=r0a4[tbl][:, dq * 512:(dq + 1) * 512]: e.tensor_tensor(
                            out=o2, in0=pba(bank)[:, 0:512],
                            in1=i1, op=ALU.add),
                        waits=[(s_pe, dn_at), (s_dve, r0a_dve), (wr_out, st5["out_wr"][ob])],
                        tag=f"d.o{s}.{dq}.{tbl}")
                    bank_dve[bank] = s_dve.count
                    o_at = s_dve.count
                    WR(lambda e, tb=tb, dq=dq, i0=ostg[ob][:]: e.dma_start(
                            out=out[tb * 128:(tb + 1) * 128, dq * 512:(dq + 1) * 512], in_=i0),
                       waits=[(s_dve, o_at)], tag=f"d.wr{s}.{dq}.{tbl}", sem=wr_out)
                    st5["out_wr"][ob] = wr_out.count
            st5["x2_pe"] = s_pe.count
            st5["x2_dve"] = s_dve.count
            st5["x2_pool"] = s_pool.count
            RA.release()
            RB.release()
            RC.release()

        TAP("ASWfull", ASW[:, :], [(w_asw, w_asw.count)], "ASWfull")
        TAP("X2Sfull", X2S[:, :], [(w_x2s, w_x2s.count)], "X2Sfull")
        P.emit("gpsimd", None,
               waits=[(s_wr, s_wr.count), (wr_out, wr_out.count), (w_asw, w_asw.count),
                      (w_x2s, w_x2s.count), (w_yfs, w_yfs.count), (w_T, w_T.count)],
               inc=None, tag="final")
        P.check()

        with nc.Block() as block:
            def replay(engname):
                def go(eng):
                    for op in P.ops[engname]:
                        for sem, v in op.waits:
                            eng.wait_ge(sem.h, v)
                        inst = op.fn(eng) if op.fn is not None else None
                        for sem, n in op.incs:
                            if inst is None:
                                raise RuntimeError(f"inc without instruction at {op.tag}")
                            inst.then_inc(sem.h, n)
                return go
            block.sync(replay("sync"))
            block.vector(replay("vector"))
            block.scalar(replay("scalar"))
            block.tensor(replay("tensor"))
            block.gpsimd(replay("gpsimd"))

    return nc


def _prep_inputs(inputs):
    x = np.ascontiguousarray(np.asarray(inputs["L0_attn_x"], dtype=np.float32))
    base = dict(
        w_qkv=np.ascontiguousarray(np.asarray(inputs["L0_attn_w_qkv"], dtype=np.float32)),
        w_sw=np.ascontiguousarray(np.asarray(inputs["L0_attn_w_swiglu"], dtype=np.float32)),
        w_out=np.ascontiguousarray(np.asarray(inputs["L0_attn_w_out"], dtype=np.float32)),
        w_up=np.ascontiguousarray(np.asarray(inputs["L0_ffn_W_up"], dtype=np.float32)),
        w_gate=np.ascontiguousarray(np.asarray(inputs["L0_ffn_W_gate"], dtype=np.float32)),
        w_down=np.ascontiguousarray(np.asarray(inputs["L0_ffn_W_down"], dtype=np.float32)),
        su_ug=np.ascontiguousarray(np.asarray(inputs["L0_ffn_SU_ug"], dtype=np.float32).reshape(16, 128).T),
        su_dp=np.ascontiguousarray(np.asarray(inputs["L0_ffn_SU_dp"], dtype=np.float32).reshape(64, 128).T),
        wsu=np.ascontiguousarray(np.asarray(inputs["L0_ffn_Wscale_ug"], dtype=np.float32)[:DFF].reshape(64, 128).T),
        wsg=np.ascontiguousarray(np.asarray(inputs["L0_ffn_Wscale_ug"], dtype=np.float32)[DFF:].reshape(64, 128).T),
        wsd=np.ascontiguousarray(np.asarray(inputs["L0_ffn_Wscale_dp"], dtype=np.float32).reshape(16, 128).T),
    )
    base.update(host_constants())
    return x, base


def kernel(**inputs):
    x, base = _prep_inputs(inputs)
    nc = build_nc()
    in_maps = []
    for c in range(8):
        m = dict(base)
        m["x"] = np.ascontiguousarray(x[c])
        in_maps.append(m)
    res = run_bass_kernel_spmd(nc, in_maps, core_ids=list(range(8)))
    return np.stack([res.results[c]["out"] for c in range(8)], axis=0)
